# version 24
# baseline (speedup 1.0000x reference)
import math
import numpy as np
import concourse.bass as bass
import concourse.mybir as mybir
from concourse.bass_utils import run_bass_kernel_spmd

F32 = mybir.dt.float32
BF16 = mybir.dt.bfloat16
I32 = mybir.dt.int32
AF = mybir.ActivationFunctionType
AL = mybir.AluOpType
AX = mybir.AxisListType.X

SEQ = 4096
DM = 4096
EPS = 1e-5
SCALE = 1.0 / math.sqrt(128.0)
GSCALE = 2.0
NE = 28

O_KCVC = 17408
O_Q = 50176
O_ACC = 82944
O_X1 = 115712
O_XU = 148480
O_UTM = 164864
O_KV = 181248
O_VTM = 185344
O_PTD = 189440
O_WS = 197632
O_WP1 = 206336
O_M01 = 215040
O_IOTA = 219136
O_THR = 221184
O_HSCT = 221440
O_ZAC = 223488
O_ONES = 225536
O_STAT = 225792
O_LOGTM = 226176
O_GSC = 226432
O_HSC = 226688
O_GST = 227712
O_GWP = 227840
O_E16 = 227968


class Sem:
    __slots__ = ("h", "v")

    def __init__(self, nc, name):
        self.h = nc.alloc_semaphore(name)
        self.v = 0

    def inc(self, ins, k=1):
        ins.then_inc(self.h, k)
        self.v += k
        return self.v


def build(nc, do_b=True, do_c=True, dbg=False, dbg_stage=None):
    X = nc.dram_tensor("X", [SEQ, DM], F32, kind="ExternalInput").ap()
    XOWN = nc.dram_tensor("XOWN", [1024, DM], F32, kind="ExternalInput").ap()
    THRD = nc.dram_tensor("THRD", [128, 64], F32, kind="ExternalInput").ap()
    WKVD4 = nc.dram_tensor("WKVD4", [4, 128, 4096], F32, kind="ExternalInput").ap()
    WQ4 = nc.dram_tensor("WQ4", [32, 128, 4096], F32, kind="ExternalInput").ap()
    WKUDE = nc.dram_tensor("WKUDE", [128, 4 * 1024], F32, kind="ExternalInput").ap()
    WKUDO = nc.dram_tensor("WKUDO", [128, 4 * 1024], F32, kind="ExternalInput").ap()
    WVUDE = nc.dram_tensor("WVUDE", [128, 4 * 1024], F32, kind="ExternalInput").ap()
    WVUDO = nc.dram_tensor("WVUDO", [128, 4 * 1024], F32, kind="ExternalInput").ap()
    WO4 = nc.dram_tensor("WO4", [32, 128, 4096], F32, kind="ExternalInput").ap()
    BWBM4 = nc.dram_tensor("BWBM4", [32, 128, 4352], F32, kind="ExternalInput").ap()
    GWT = nc.dram_tensor("GWT", [128, 896], F32, kind="ExternalInput").ap()
    ATP = nc.dram_tensor("ATP", [32, 128, 256], F32, kind="ExternalInput").ap()
    OUT = nc.dram_tensor("OUT", [1024, DM], F32, kind="ExternalOutput").ap()
    WQB = nc.dram_tensor("WQB", [32, 128, 4096], BF16, kind="Internal").ap()
    WOB = nc.dram_tensor("WOB", [32, 128, 4096], BF16, kind="Internal").ap()
    BWBMB = nc.dram_tensor("BWBMB", [32, 128, 4352], BF16, kind="Internal").ap()
    ATPB = nc.dram_tensor("ATPB", [32, 128, 256], BF16, kind="Internal").ap()
    WKVDB = nc.dram_tensor("WKVDB", [4, 32, 128, 128], BF16, kind="Internal").ap()
    DBGT = None
    if dbg:
        DBGT = nc.dram_tensor("DBG", [512, DM], F32, kind="ExternalOutput").ap()

    A = lambda name, shape, dt, off: nc.alloc_sbuf_tensor_at(name, shape, dt, offset=off)
    KCVC = A("KCVC", [128, 4 * 4096], BF16, O_KCVC)
    QFM = A("QFM", [128, 32 * 512], BF16, O_Q)
    ACC = A("ACC", [128, 16 * 512], F32, O_ACC)
    UFMA = A("UFMA", [128, 32 * 512], BF16, O_ACC)
    OUT2T = A("OUT2T", [128, 32 * 512], BF16, O_ACC)
    FFNT = A("FFNT", [128, 32 * 512], BF16, O_ACC)
    X1 = A("X1", [128, 4 * 4096], BF16, O_X1)
    XA = A("XA", [128, 2 * 4096], BF16, O_XU)
    XA2 = A("XA2", [128, 4 * 4096], BF16, O_Q)
    UFMA2 = A("UFMA2", [128, 32 * 512], BF16, O_X1)
    UTM = A("UTM", [128, 2 * 4096], BF16, O_UTM)
    OUT2TM = A("OUT2TM", [128, 4 * 4096], BF16, O_XU)
    FFNTM = A("FFNTM", [128, 4 * 4096], BF16, O_XU)
    KFM = A("KFM", [128, 4 * 512], BF16, O_KV)
    VTM = A("VTM", [128, 4 * 512], BF16, O_VTM)
    PTD = A("PTD", [128, 4 * 2048], BF16, O_XU)
    Y2 = A("Y2", [128, 4096], BF16, O_PTD)
    Y2B = A("Y2B", [128, 4096], BF16, O_KV)
    OSTG = A("OSTG", [128, 2048], F32, O_KV)
    OSTG2 = A("OSTG2", [128, 2048], F32, O_Q)
    WP = [A("WPa", [128, 34 * 128], BF16, O_WS), A("WPb", [128, 34 * 128], BF16, O_WP1)]
    WKUS = A("WKUS", [128, 4 * 1024], BF16, O_WS)
    WVUS = A("WVUS", [128, 4 * 1024], BF16, O_WP1)
    WKUSO = A("WKUSO", [128, 4 * 1024], BF16, O_UTM)
    WVUSO = A("WVUSO", [128, 4 * 1024], BF16, O_UTM + 8192)
    M01 = A("M01", [128, 4 * 512], BF16, O_M01)
    IND16 = A("IND16", [16, 16 * 128], BF16, O_M01)
    IOTA = A("IOTA", [128, 512], F32, O_IOTA)
    IOTAI = A("IOTAI", [128, 512], I32, O_HSCT)
    THR = A("THR", [128, 64], F32, O_THR)
    HSCT = A("HSCT", [128, 2 * 512], BF16, O_HSCT)
    ZAC = A("ZAC", [128, 512], F32, O_ZAC)
    LOGPAD = A("LOGPAD", [32, 512], BF16, O_ZAC)
    ONES = A("ONES", [128, 128], BF16, O_ONES)
    STAT = A("STAT", [128, 96], F32, O_STAT)
    LOGTM = A("LOGTM", [128, 128], BF16, O_LOGTM)
    GSC = A("GSC", [128, 4 * 32], BF16, O_GSC)
    HSC = A("HSC", [128, 2 * 256], BF16, O_HSC)
    RZB = A("RZB", [16, 512], BF16, O_HSC)
    GST = A("GST", [128, 32], F32, O_GST)
    GWP = A("GWP", [128, 2 * 32], BF16, O_GWP)
    GWPF = A("GWPF", [128, 32 * 28], BF16, O_M01)
    BSTAT = A("BSTAT", [128, 96], F32, 228480)
    E16 = A("E16", [128, 256], BF16, O_E16)

    PSab = [nc.alloc_psum_tensor("psa", [128, 512], F32),
            nc.alloc_psum_tensor("psb", [128, 512], F32)]
    PSS = nc.alloc_psum_tensor("pss", [128, 2048], F32)
    PSZ = nc.alloc_psum_tensor("psz", [128, 512], F32)
    PSV = nc.alloc_psum_tensor("psv", [128, 512], F32)

    TEN, VEC, SCA, GPS, SYN = nc.tensor, nc.vector, nc.scalar, nc.gpsimd, nc.sync

    tS = Sem(nc, "tS")
    vS = Sem(nc, "vS")
    sS = Sem(nc, "sS")
    gS = Sem(nc, "gS")
    sXA = [Sem(nc, "sXA%d" % i) for i in range(6)]
    sWPs = [Sem(nc, "sWP0"), Sem(nc, "sWP1")]
    sWPh = [Sem(nc, "sWPh0"), Sem(nc, "sWPh1")]
    sTU = [Sem(nc, "sTU0"), Sem(nc, "sTU1")]
    sTH = [Sem(nc, "sTH0"), Sem(nc, "sTH1")]
    sTP = Sem(nc, "sTP")
    sTHR = Sem(nc, "sTHR")
    sWKV = Sem(nc, "sWKV")
    sGW = [Sem(nc, "sGW0"), Sem(nc, "sGW1")]
    sAP = [Sem(nc, "sAP0"), Sem(nc, "sAP1")]
    sST = Sem(nc, "sST")
    sSTp = [Sem(nc, "sSTa"), Sem(nc, "sSTb")]
    sRES = Sem(nc, "sRES")
    sCVQ = Sem(nc, "sCVQ")
    sCVO = Sem(nc, "sCVO")
    sCVB = Sem(nc, "sCVB")
    sCVA = Sem(nc, "sCVA")
    sCVK = Sem(nc, "sCVK")

    def W(eng, *sv):
        for s, v in sv:
            if v > 0:
                eng.wait_ge(s.h, v)

    GPS.iota(IOTAI[:], pattern=[[1, 512]], base=0, channel_multiplier=-1)
    GPS.drain()
    GPS.tensor_copy(IOTA[:], IOTAI[:])
    GPS.iota(E16[:].rearrange("p (h i) -> p h i", h=16, i=16),
             pattern=[[-1, 16], [1, 16]], base=0, channel_multiplier=0,
             allow_small_or_imprecise_dtypes=True)
    iota_done = gS.inc(GPS.drain())
    thr_ld = sTHR.inc(GPS.dma_start(out=THR[:], in_=THRD[:]), 16)
    VEC.memset(ONES[:], 1.0)
    VEC.drain()
    W(VEC, (gS, iota_done))
    VEC.tensor_single_scalar(E16[:], E16[:], 0.0, AL.is_equal)
    VEC.drain()

    ln_prev_ap = {0: 0, 1: 0}

    def ln_stats_apply(src_tile, utile, pb, waits_v=(), waits_s=()):
        b = pb * 16
        Sm = STAT[:, b + 0:b + 1]
        Sv = STAT[:, b + 1:b + 2]
        Sq = STAT[:, b + 4:b + 5]
        Svr = STAT[:, b + 5:b + 6]
        Ssd = STAT[:, b + 6:b + 7]
        Sr = STAT[:, b + 7:b + 8]
        Snmr = STAT[:, b + 8:b + 9]
        BS = BSTAT[:, pb * 48:(pb + 1) * 48]
        W(VEC, *waits_v)
        for i in range(8):
            VEC.bn_stats(BS[:, i * 6:(i + 1) * 6], src_tile[:, i * 512:(i + 1) * 512])
        VEC.drain()
        VEC.bn_aggr(STAT[:, b:b + 2], BS)
        VEC.drain()
        VEC.tensor_scalar(Svr, Sv, 1.0, EPS, AL.mult, AL.add)
        VEC.drain()
        W(VEC, (sS, ln_prev_ap[pb]))
        VEC.reciprocal(Ssd, Svr)
        VEC.drain()
        VEC.tensor_single_scalar(Sq, Sm, -1.0, AL.mult)
        var_mark = vS.inc(VEC.drain())
        W(SCA, (vS, var_mark), *waits_s)
        SCA.activation(Sr, Ssd, AF.Sqrt)
        SCA.drain()
        SCA.mul(Snmr, Sr, Sq)
        SCA.drain()
        ins = SCA.activation(utile, src_tile, AF.Identity, bias=Snmr, scale=Sr)
        ap_mark = sS.inc(ins)
        ln_prev_ap[pb] = ap_mark
        return ap_mark, var_mark

    def tr_u(dst, tspan, toff, utile, waits, pb):
        W(SYN, *waits)
        ap = dst[:].rearrange("p (c t) -> p c t", c=32, t=tspan)[:, :, toff:toff + 128]
        return sTU[pb].inc(SYN.dma_start(out=ap, in_=utile, transpose=True), 16)

    all_dma_sems = sXA + [sWPs[0], sWPs[1], sWPh[0], sWPh[1],
                    sTU[0], sTU[1], sTH[0], sTH[1],
                    sTP, sTHR, sWKV, sGW[0], sGW[1], sAP[0], sAP[1], sST, sRES,
                    sSTp[0], sSTp[1], sCVQ, sCVO, sCVB, sCVA, sCVK]

    def final_waits():
        for s in all_dma_sems:
            if s.v > 0:
                SYN.wait_ge(s.h, s.v)

    conv_jobs = []
    for i in range(32):
        conv_jobs.append((sCVQ, WQB[i], WQ4[i]))
    for i in range(32):
        conv_jobs.append((sCVO, WOB[i], WO4[i]))
    for i in range(32):
        conv_jobs.append((sCVB, BWBMB[i], BWBM4[i]))
    for i in range(32):
        conv_jobs.append((sCVA, ATPB[i], ATP[i]))
    cv_i = [0]

    def pump_conv(n):
        for _ in range(n):
            if cv_i[0] >= len(conv_jobs):
                return
            s, d, srcp = conv_jobs[cv_i[0]]
            s.inc(GPS.dma_start(out=d, in_=srcp), 16)
            cv_i[0] += 1

    Xr = X.rearrange("(q0 eight) d -> q0 eight d", eight=8)
    xu_extra = []
    free_xa = [(0, 0)] * 6

    def xa_tile(i):
        if i < 2:
            return XA[:, i * 4096:(i + 1) * 4096]
        return XA2[:, (i - 2) * 4096:(i - 1) * 4096]
    utm_free = [0, 0]
    ufma_free_t = 0
    ufma_free_v = 0
    wp_free = [0, 0]
    psab_free = [0, 0]
    ufma_pa = {0: 0, 1: 0}
    pend_xm = {}

    def pa_loads(ub):
        for t4 in range(4):
            b = (ub * 4 + t4) % 6
            sv, _ = free_xa[b]
            W(GPS, (sS, sv))
            src = Xr[t4 * 128:(t4 + 1) * 128, ub, :]
            pend_xm[t4] = sXA[b].inc(GPS.dma_start(out=xa_tile(b), in_=src), 16)

    pa_loads(0)
    for ub in range(8):
        ubuf = UFMA if ub % 2 == 0 else UFMA2
        for t4 in range(4):
            pb = (ub * 4 + t4) % 6
            pl = (ub * 4 + t4) % 2
            xm = pend_xm[t4]
            xt = xa_tile(pb)
            ut = UTM[:, pl * 4096:(pl + 1) * 4096]
            ap_mark, red_mark = ln_stats_apply(
                xt, ut, pl,
                waits_v=[(sXA[pb], xm)],
                waits_s=[(sXA[pb], xm), (sTU[pl], utm_free[pl])])
            free_xa[pb] = (ap_mark, red_mark)
            tr_u(ubuf, 512, t4 * 128, ut,
                 [(sS, ap_mark), (tS, ufma_pa[ub % 2]), (vS, ufma_free_v)], pl)
            utm_free[pl] = sTU[pl].v
        if ub < 7:
            pa_loads(ub + 1)
        tp_all = [(sTU[0], sTU[0].v), (sTU[1], sTU[1].v)]
        for cg in range(4):
            wp = cg % 2
            W(GPS, (tS, wp_free[wp]))
            wm = sWPs[wp].inc(GPS.dma_start(out=WP[wp][:, 0:32 * 128], in_=WKVD4[cg]), 16)
            pp = cg % 2
            W(TEN, (sWPs[wp], wm), (vS, psab_free[pp]), *tp_all)
            for c in range(32):
                ins = TEN.matmul(PSab[pp][:], WP[wp][:, c * 128:(c + 1) * 128],
                                 ubuf[:, c * 512:(c + 1) * 512],
                                 start=(c == 0), stop=(c == 31))
            mm_mark = tS.inc(ins)
            wp_free[wp] = mm_mark
            if cg == 3:
                ufma_pa[ub % 2] = mm_mark
            if cg == 1:
                W(GPS, (tS, wp_free[0]))
                pump_conv(12)
            W(VEC, (tS, mm_mark))
            psab_free[pp] = vS.inc(VEC.tensor_copy(
                KCVC[:, cg * 4096 + ub * 512:cg * 4096 + (ub + 1) * 512], PSab[pp][:]))
    ufma_free_t = ufma_pa[0]
    x1_pa = [(tS, ufma_pa[1])]

    if not do_b:
        stm = 0
        for cc in range(4):
            W(VEC, (sST, stm))
            VEC.tensor_copy(OSTG[:], KCVC[:, cc * 4096:(cc + 1) * 4096])
            dm = vS.inc(VEC.drain())
            W(SYN, (vS, dm))
            stm = sST.inc(SYN.dma_start(out=DBGT[cc * 128:(cc + 1) * 128, :], in_=OSTG[:]), 16)
            SYN.wait_ge(sST.h, stm)
        final_waits()
        return

    kcvc_ready = vS.v
    ostg_all = []
    kv_free_t = 0
    y2_free_t = 0
    x1_free = 0
    qfm_free_t = 0

    for ci in range(2):
        nsl = 4 if ci == 0 else 8
        xbase = ci * 512
        for tt in range(4):
            pb = (32 + ci * 4 + tt) % 6
            pl = tt % 2
            sv, vv = free_xa[pb]
            W(GPS, (sS, sv), (tS, kv_free_t), *xu_extra, *ostg_all)
            src = XOWN[xbase + tt * 128: xbase + (tt + 1) * 128, :]
            xm = sXA[pb].inc(GPS.dma_start(out=xa_tile(pb), in_=src), 16)
            xt = xa_tile(pb)
            ut = UTM[:, pl * 4096:(pl + 1) * 4096]
            ap_mark, red_mark = ln_stats_apply(
                xt, ut, pl,
                waits_v=[(sXA[pb], xm)],
                waits_s=[(sXA[pb], xm), (sTU[pl], utm_free[pl])] + xu_extra)
            free_xa[pb] = (ap_mark, red_mark)
            tr_u(UFMA, 512, tt * 128, ut,
                 [(sS, ap_mark), (tS, ufma_free_t), (vS, ufma_free_v)], pl)
            utm_free[pl] = sTU[pl].v
        tp_all = [(sTU[0], sTU[0].v), (sTU[1], sTU[1].v)]
        pump_conv(len(conv_jobs))
        for hdc in range(32):
            wp = hdc % 2
            W(SYN, (tS, wp_free[wp]), (sCVQ, sCVQ.v))
            wm = sWPh[wp].inc(SYN.dma_start(out=WP[wp][:, 0:32 * 128], in_=WQB[hdc]), 16)
            pp = hdc % 2
            W(TEN, (sWPh[wp], wm), (vS, psab_free[pp]), *tp_all)
            for c in range(32):
                ins = TEN.matmul(PSab[pp][:], WP[wp][:, c * 128:(c + 1) * 128],
                                 UFMA[:, c * 512:(c + 1) * 512],
                                 start=(c == 0), stop=(c == 31))
            mm_mark = tS.inc(ins)
            wp_free[wp] = mm_mark
            if hdc == 31:
                ufma_free_t = mm_mark
            W(VEC, (tS, mm_mark))
            if hdc == 0:
                W(VEC, *ostg_all)
            psab_free[pp] = vS.inc(VEC.tensor_copy(QFM[:, hdc * 512:(hdc + 1) * 512],
                                                   PSab[pp][:]))
        q_ready = vS.v

        W(GPS, (tS, max(wp_free)))
        sWKV.inc(GPS.dma_start(out=WKUS[:], in_=WKUDE[:]), 16)
        sWKV.inc(GPS.dma_start(out=WKUSO[:], in_=WKUDO[:]), 16)
        sWKV.inc(GPS.dma_start(out=WVUS[:], in_=WVUDE[:]), 16)
        wkv_ld = sWKV.inc(GPS.dma_start(out=WVUSO[:], in_=WVUDO[:]), 16)
        W(GPS, (vS, x1_free), *x1_pa)
        x1_pa = []
        xl_last = 0
        for tt in range(4):
            src = XOWN[xbase + tt * 128: xbase + (tt + 1) * 128, :]
            xl_last = sRES.inc(GPS.dma_start(
                out=X1[:, tt * 4096:(tt + 1) * 4096], in_=src), 16)

        for hg in range(2):
            for sl in range(nsl):
                W(VEC, (sTHR, thr_ld), (gS, iota_done), *ostg_all)
                for kt in range(4):
                    col = ci * 32 + sl * 4 + kt
                    VEC.tensor_single_scalar(M01[:, kt * 512:(kt + 1) * 512], IOTA[:],
                                             THR[:, col:col + 1], AL.is_ge)
                mask_mark = vS.inc(VEC.drain())
                base = (sl // 2) * 4096
                wkp = WKUS if sl % 2 == 0 else WKUSO
                wvp = WVUS if sl % 2 == 0 else WVUSO
                for kv4 in range(4):
                    hk2 = hg * 4 + kv4
                    pp = kv4 % 2
                    W(TEN, (sWKV, wkv_ld), (vS, max(psab_free[pp], kcvc_ready)))
                    for u in range(4):
                        ins = TEN.matmul(
                            PSab[pp][:],
                            wkp[:, u * 1024 + hk2 * 128: u * 1024 + (hk2 + 1) * 128],
                            KCVC[:, base + u * 512: base + (u + 1) * 512],
                            start=(u == 0), stop=(u == 3))
                    km = tS.inc(ins)
                    W(VEC, (tS, km))
                    psab_free[pp] = vS.inc(VEC.tensor_copy(
                        KFM[:, kv4 * 512:(kv4 + 1) * 512], PSab[pp][:]))
                for kt in range(4):
                    pp = kt % 2
                    W(TEN, (vS, psab_free[pp]))
                    for u in range(4):
                        ins = TEN.matmul(
                            PSab[pp][:],
                            KCVC[:, base + (u + 4) * 512 + kt * 128:
                                    base + (u + 4) * 512 + (kt + 1) * 128],
                            wvp[:, u * 1024 + hg * 512: u * 1024 + (hg + 1) * 512],
                            start=(u == 0), stop=(u == 3))
                    vm = tS.inc(ins)
                    wkus_last_t = vm
                    W(VEC, (tS, vm))
                    psab_free[pp] = vS.inc(VEC.tensor_copy(
                        VTM[:, kt * 512:(kt + 1) * 512], PSab[pp][:]))
                upv_done = vS.v
                if dbg_stage == "kv0" and ci == 0 and hg == 0 and sl == 0:
                    stm = 0
                    DSTG = ACC[:, 0:2048]
                    for (src_t, row) in ((KFM, 0), (VTM, 128)):
                        W(VEC, (sST, stm), (vS, upv_done))
                        VEC.tensor_copy(DSTG, src_t[:, 0:2048])
                        dm = vS.inc(VEC.drain())
                        W(SYN, (vS, dm))
                        stm = sST.inc(SYN.dma_start(
                            out=DBGT[row:row + 128, 0:2048], in_=DSTG), 16)
                        SYN.wait_ge(sST.h, stm)
                    final_waits()
                    return

                pss_free = [0, 0]
                ptfree = [0, 0, 0, 0]
                pends = []
                for h4 in range(19):
                    if len(pends) == 3 or (h4 >= 16 and pends):
                        ph4, pptp, pkv4, pmult = pends.pop(0)
                        avb = ph4 % 2
                        W(TEN, (vS, max(pmult, psab_free[avb])))
                        for kt in range(4):
                            TEN.matmul(PSZ[0:16, :],
                                       E16[:, ph4 * 16:(ph4 + 1) * 16],
                                       PTD[:, pptp * 2048 + kt * 512:
                                           pptp * 2048 + (kt + 1) * 512],
                                       start=(sl == 0 and ph4 == 0 and kt == 0),
                                       stop=(sl == nsl - 1 and ph4 == 15 and kt == 3))
                        for kt in range(4):
                            ins = TEN.matmul(
                                PSab[avb][:],
                                VTM[:, kt * 512 + pkv4 * 128: kt * 512 + (pkv4 + 1) * 128],
                                PTD[:, pptp * 2048 + kt * 512: pptp * 2048 + (kt + 1) * 512],
                                start=(kt == 0), stop=(kt == 3))
                        zav_mark = tS.inc(ins)
                        ptfree[pptp] = zav_mark
                        if ph4 == 15 and sl == nsl - 1 and hg == 1:
                            kv_free_t = zav_mark
                            y2_free_t = zav_mark
                        W(VEC, (tS, zav_mark))
                        if sl == 0:
                            ins = VEC.tensor_copy(ACC[:, ph4 * 512:(ph4 + 1) * 512],
                                                  PSab[avb][:])
                        else:
                            ins = VEC.tensor_tensor(ACC[:, ph4 * 512:(ph4 + 1) * 512],
                                                    ACC[:, ph4 * 512:(ph4 + 1) * 512],
                                                    PSab[avb][:], AL.add)
                        psab_free[avb] = vS.inc(ins)
                    if h4 < 16:
                        h = hg * 16 + h4
                        kv4 = h4 // 4
                        ptp = h4 % 4
                        pfs = ptfree[ptp]
                        sc_half = []
                        for hf2 in range(2):
                            W(TEN, (sS, pss_free[hf2]), (vS, upv_done if h4 == 0 else 0))
                            for kt in (hf2 * 2, hf2 * 2 + 1):
                                ins = TEN.matmul(
                                    PSS[:, kt * 512:(kt + 1) * 512],
                                    KFM[:, kv4 * 512 + kt * 128: kv4 * 512 + (kt + 1) * 128],
                                    QFM[:, h * 512:(h + 1) * 512],
                                    start=True, stop=True)
                            sc_half.append(tS.inc(ins))
                        for hf2 in range(2):
                            W(SCA, (tS, max(sc_half[hf2], pfs)))
                            ins = SCA.activation(
                                PTD[:, ptp * 2048 + hf2 * 1024: ptp * 2048 + (hf2 + 1) * 1024],
                                PSS[:, hf2 * 1024:(hf2 + 1) * 1024],
                                AF.Exp, scale=SCALE)
                            pss_free[hf2] = sS.inc(ins)
                        W(VEC, (sS, pss_free[1]), (vS, mask_mark))
                        pm = vS.inc(VEC.tensor_tensor(PTD[:, ptp * 2048:(ptp + 1) * 2048],
                                                      PTD[:, ptp * 2048:(ptp + 1) * 2048],
                                                      M01[:], AL.mult))
                        pends.append((h4, ptp, kv4, pm))
            W(GPS, (vS, vS.v))
            GPS.iota(IND16[:].rearrange("p (h m) -> p h m", h=16, m=128),
                     pattern=[[1, 16], [0, 128]], base=0, channel_multiplier=-1,
                     allow_small_or_imprecise_dtypes=True)
            ind_g = gS.inc(GPS.drain())
            W(VEC, (gS, ind_g), (tS, zav_mark))
            VEC.tensor_single_scalar(IND16[:], IND16[:], 0.0, AL.is_equal)
            VEC.drain()
            VEC.reciprocal(ZAC[0:16, :], PSZ[0:16, :])
            VEC.drain()
            VEC.tensor_copy(RZB[:], ZAC[0:16, :])
            div_mark = vS.inc(VEC.drain())
            for h4 in range(16):
                h = hg * 16 + h4
                pp = h4 % 2
                W(TEN, (vS, max(div_mark, psab_free[pp])))
                ins = TEN.matmul(PSab[pp][:], IND16[:, h4 * 128:(h4 + 1) * 128],
                                 RZB[:], start=True, stop=True)
                rm = tS.inc(ins)
                W(VEC, (tS, rm))
                psab_free[pp] = vS.inc(VEC.tensor_tensor(QFM[:, h * 512:(h + 1) * 512],
                                                         ACC[:, h4 * 512:(h4 + 1) * 512],
                                                         PSab[pp][:], AL.mult))
        ufma_free_v = vS.v
        attn_ready = vS.v
        pss_free_s = sS.v
        W(GPS, (vS, attn_ready), (tS, tS.v))
        gwf_ld = sGW[0].inc(GPS.dma_start(out=GWPF[:], in_=GWT[:]), 16)
        if dbg_stage == "attnT":
            stm = 0
            for cc in range(4):
                W(VEC, (sST, stm))
                VEC.tensor_copy(OSTG[:], QFM[:, cc * 4096 + 0: cc * 4096 + 2048])
                dm = vS.inc(VEC.drain())
                W(SYN, (vS, dm))
                stm = sST.inc(SYN.dma_start(
                    out=DBGT[cc * 128:(cc + 1) * 128, 0:2048], in_=OSTG[:]), 16)
                SYN.wait_ge(sST.h, stm)
                W(VEC, (sST, stm))
                VEC.tensor_copy(OSTG[:], QFM[:, cc * 4096 + 2048: cc * 4096 + 4096])
                dm = vS.inc(VEC.drain())
                W(SYN, (vS, dm))
                stm = sST.inc(SYN.dma_start(
                    out=DBGT[cc * 128:(cc + 1) * 128, 2048:4096], in_=OSTG[:]), 16)
                SYN.wait_ge(sST.h, stm)
            final_waits()
            return

        wo_base_free = max(max(wp_free), wkus_last_t)
        evac_marks = []
        xu_free_s = sS.v

        def tp_out2(oc2):
            W(SYN, (vS, evac_marks[oc2]), (sS, xu_free_s))
            dstw = OUT2TM[:].rearrange("p (t f) -> p t f", t=4, f=4096)[
                :, :, oc2 * 128:(oc2 + 1) * 128]
            sTP.inc(SYN.dma_start(out=dstw, in_=OUT2T[:, oc2 * 512:(oc2 + 1) * 512],
                                  transpose=True), 16)

        for oc2 in range(32):
            wp = oc2 % 2
            W(SYN, (tS, max(wp_free[wp], wo_base_free)), (sCVO, sCVO.v))
            wm = sWPh[wp].inc(SYN.dma_start(out=WP[wp][:, 0:32 * 128], in_=WOB[oc2]), 16)
            pp = oc2 % 2
            W(TEN, (sWPh[wp], wm), (vS, max(psab_free[pp], attn_ready)))
            for hdc in range(32):
                ins = TEN.matmul(PSab[pp][:], WP[wp][:, hdc * 128:(hdc + 1) * 128],
                                 QFM[:, hdc * 512:(hdc + 1) * 512],
                                 start=(hdc == 0), stop=(hdc == 31))
            mm_mark = tS.inc(ins)
            wp_free[wp] = mm_mark
            if oc2 == 31:
                qfm_free_t = mm_mark
            W(VEC, (tS, mm_mark))
            em = vS.inc(VEC.tensor_copy(OUT2T[:, oc2 * 512:(oc2 + 1) * 512], PSab[pp][:]))
            psab_free[pp] = em
            evac_marks.append(em)
            if oc2 >= 2:
                tp_out2(oc2 - 2)
        tp_out2(30)
        tp_out2(31)
        tp_o2 = sTP.v
        y2r = [0, 0]
        add_marks = []
        for tt in range(4):
            par = tt % 2
            ybuf = Y2 if par == 0 else Y2B
            o2t = OUT2TM[:, tt * 4096:(tt + 1) * 4096]
            ap_mark, _ = ln_stats_apply(
                o2t, ybuf[:], par,
                waits_v=[(sTP, tp_o2)],
                waits_s=[(sTP, tp_o2), (tS, kv_free_t), (vS, y2r[par])])
            W(VEC, (sS, ap_mark), (sRES, xl_last))
            ins = VEC.tensor_tensor(X1[:, tt * 4096:(tt + 1) * 4096],
                                    X1[:, tt * 4096:(tt + 1) * 4096], ybuf[:], AL.add)
            y2r[par] = vS.inc(ins)
            add_marks.append(y2r[par])
        res_free_v = add_marks[3]

        if not do_c:
            stm = 0
            for tt in range(4):
                for hf in range(2):
                    W(VEC, (sST, stm), (vS, res_free_v))
                    VEC.tensor_copy(OSTG[:],
                                    X1[:, tt * 4096 + hf * 2048: tt * 4096 + (hf + 1) * 2048])
                    dm = vS.inc(VEC.drain())
                    W(SYN, (vS, dm))
                    stm = sST.inc(SYN.dma_start(
                        out=DBGT[tt * 128:(tt + 1) * 128, hf * 2048:(hf + 1) * 2048],
                        in_=OSTG[:]), 16)
                    SYN.wait_ge(sST.h, stm)
            final_waits()
            return

        for tt in range(4):
            par = tt % 2
            ybuf = Y2 if par == 0 else Y2B
            x1t = X1[:, tt * 4096:(tt + 1) * 4096]
            ap_mark, _ = ln_stats_apply(
                x1t, ybuf[:], par,
                waits_v=[(vS, add_marks[tt])],
                waits_s=[(sTU[par], sTU[par].v),
                         (vS, max(y2r[par], add_marks[tt]))] + ostg_all)
            tr_u(QFM, 512, tt * 128, ybuf[:], [(sS, ap_mark), (tS, qfm_free_t)], par)
        tp_u1 = [(sTU[0], sTU[0].v), (sTU[1], sTU[1].v)]
        for c in range(32):
            if c == 0:
                W(TEN, (sGW[0], gwf_ld), (vS, psab_free[0]), *tp_u1)
            ins = TEN.matmul(PSab[0][0:28, :], GWPF[:, c * 28:(c + 1) * 28],
                             QFM[:, c * 512:(c + 1) * 512], start=(c == 0), stop=(c == 31))
        gate_mm = tS.inc(ins)
        W(VEC, (tS, gate_mm))
        VEC.memset(LOGPAD[:], -30000.0)
        VEC.drain()
        VEC.tensor_copy(LOGPAD[0:28, :], PSab[0][0:28, :])
        lg_mark = vS.inc(VEC.drain())
        psab_free[0] = lg_mark
        W(SYN, (vS, lg_mark))
        tp_log = sTP.inc(SYN.dma_start(
            out=LOGTM[:].rearrange("p (a b) -> p a b", a=4, b=32),
            in_=LOGPAD[:], transpose=True), 16)
        for tt in range(4):
            L28 = LOGTM[:, tt * 32: tt * 32 + 28]
            m1 = GST[:, 0:1]
            m2 = GST[:, 1:2]
            nm1 = GST[:, 2:3]
            zg = GST[:, 3:4]
            rg = GST[:, 4:5]
            msk = STAT[:, 32:60]
            l2 = STAT[:, 64:92]
            W(VEC, (sTP, tp_log))
            VEC.reduce_max(m1, L28, axis=AX)
            VEC.drain()
            VEC.tensor_single_scalar(msk, L28, m1, AL.is_ge)
            VEC.drain()
            VEC.scalar_tensor_tensor(l2, msk, -30000.0, L28, AL.mult, AL.add)
            VEC.drain()
            VEC.reduce_max(m2, l2, axis=AX)
            VEC.drain()
            VEC.tensor_single_scalar(msk, L28, m2, AL.is_ge)
            VEC.drain()
            VEC.tensor_single_scalar(nm1, m1, -1.0, AL.mult)
            g1 = vS.inc(VEC.drain())
            W(SCA, (vS, g1))
            ge = sS.inc(SCA.activation(l2, L28, AF.Exp, bias=nm1, scale=1.0))
            W(VEC, (sS, ge))
            VEC.tensor_tensor(l2, l2, msk, AL.mult)
            VEC.drain()
            VEC.reduce_sum(zg, l2, axis=AX)
            VEC.drain()
            VEC.reciprocal(rg, zg)
            VEC.drain()
            VEC.tensor_scalar(GSC[:, tt * 32: tt * 32 + 28], l2, rg, GSCALE,
                              AL.mult, AL.mult)
            vS.inc(VEC.drain())
        gate_done = vS.v
        ap_free = [wp_free[0], wp_free[0]]
        for c in range(32):
            ap2 = c % 2
            W(SYN, (tS, ap_free[ap2]), (sCVA, sCVA.v))
            am = sAP[ap2].inc(SYN.dma_start(out=WP[0][:, ap2 * 256:(ap2 + 1) * 256],
                                            in_=ATPB[c]), 16)
            W(TEN, (sAP[ap2], am))
            if c == 0:
                W(TEN, (sS, pss_free_s))
            for ts in range(4):
                ins = TEN.matmul(PSS[:, ts * 512: ts * 512 + 256],
                                 QFM[:, c * 512 + ts * 128: c * 512 + (ts + 1) * 128],
                                 WP[0][:, ap2 * 256:(ap2 + 1) * 256],
                                 start=(c == 0), stop=(c == 31))
            ap_free[ap2] = tS.inc(ins)
        h_mm = tS.v
        hsc_free = [0, 0]
        hsc_marks = []
        for ts in range(4):
            hb = ts % 2
            gbx = GSC[:].rearrange("p (t e) -> p t e", t=4, e=32)[:, ts, 0:28]
            gbx = gbx.rearrange("p (e one) -> p e one", one=1).broadcast_to([128, 28, 8])
            W(VEC, (tS, h_mm), (sTH[hb], hsc_free[hb]), (vS, gate_done))
            VEC.tensor_tensor(
                HSC[:].rearrange("p (b e r) -> p b e r", b=2, e=32, r=8)[:, hb, 0:28, :],
                PSS[:].rearrange("p (t f) -> p t f", t=4, f=512)[:, ts, 0:224]
                      .rearrange("p (e r) -> p e r", e=28, r=8),
                gbx, AL.mult)
            VEC.drain()
            VEC.tensor_copy(HSC[:, hb * 256 + 224: hb * 256 + 256],
                            PSS[:, ts * 512 + 224: ts * 512 + 256])
            hm = vS.inc(VEC.drain())
            hsc_marks.append(hm)
            W(SYN, (vS, hm))
            dstw = HSCT[:].rearrange("p (cc t) -> p cc t", cc=2, t=512)[
                :, :, ts * 128:(ts + 1) * 128]
            hsc_free[hb] = sTH[hb].inc(SYN.dma_start(
                out=dstw, in_=HSC[:, hb * 256:(hb + 1) * 256], transpose=True), 16)
        hsc_last_v = vS.v
        tp_h = [(sTH[0], sTH[0].v), (sTH[1], sTH[1].v)]
        evac_marks = []
        bl_base_free = max(max(wp_free), ap_free[0], ap_free[1])
        xu_free_s = sS.v

        def tp_ffn(oc2):
            W(SYN, (vS, evac_marks[oc2]), (sS, xu_free_s))
            dstw = FFNTM[:].rearrange("p (t f) -> p t f", t=4, f=4096)[
                :, :, oc2 * 128:(oc2 + 1) * 128]
            sTP.inc(SYN.dma_start(out=dstw, in_=FFNT[:, oc2 * 512:(oc2 + 1) * 512],
                                  transpose=True), 16)

        for oc2 in range(32):
            wp = oc2 % 2
            W(SYN, (tS, max(wp_free[wp], bl_base_free)), (sCVB, sCVB.v))
            wm = sWPh[wp].inc(SYN.dma_start(out=WP[wp][:], in_=BWBMB[oc2]), 16)
            pp = oc2 % 2
            W(TEN, (sWPh[wp], wm), (vS, psab_free[pp]))
            for c in range(32):
                TEN.matmul(PSab[pp][:], WP[wp][:, c * 128:(c + 1) * 128],
                           QFM[:, c * 512:(c + 1) * 512], start=(c == 0), stop=False)
            if oc2 == 0:
                W(TEN, *tp_h)
            for cc in range(2):
                ins = TEN.matmul(PSab[pp][:], WP[wp][:, (32 + cc) * 128:(33 + cc) * 128],
                                 HSCT[:, cc * 512:(cc + 1) * 512],
                                 start=False, stop=(cc == 1))
            mm_mark = tS.inc(ins)
            wp_free[wp] = mm_mark
            if oc2 == 31:
                qfm_free_t = mm_mark
            W(VEC, (tS, mm_mark))
            em = vS.inc(VEC.tensor_copy(FFNT[:, oc2 * 512:(oc2 + 1) * 512], PSab[pp][:]))
            psab_free[pp] = em
            evac_marks.append(em)
            if oc2 >= 2:
                tp_ffn(oc2 - 2)
        tp_ffn(30)
        tp_ffn(31)
        tp_f = sTP.v
        ostg_v = 0
        ap4_last = 0
        ostg_prev = [0, 0]
        for tt in range(4):
            ft = FFNTM[:, tt * 4096:(tt + 1) * 4096]
            ap_mark, _ = ln_stats_apply(
                ft, Y2[:], tt % 2,
                waits_v=[(sTP, tp_f), (vS, ostg_v)],
                waits_s=[(sTP, tp_f), (vS, ostg_v)])
            ap4_last = ap_mark
            for hf in range(2):
                ob = (tt * 2 + hf) % 2
                stg = OSTG if ob == 0 else OSTG2
                W(VEC, (sS, ap_mark), (sSTp[ob], ostg_prev[ob]))
                ins = VEC.tensor_tensor(
                    stg[:],
                    X1[:, tt * 4096 + hf * 2048: tt * 4096 + (hf + 1) * 2048],
                    Y2[:, hf * 2048:(hf + 1) * 2048], AL.add)
                ostg_v = vS.inc(ins)
                W(SYN, (vS, ostg_v))
                ostg_prev[ob] = sSTp[ob].inc(SYN.dma_start(
                    out=OUT[xbase + tt * 128: xbase + (tt + 1) * 128,
                            hf * 2048:(hf + 1) * 2048], in_=stg[:]), 16)
        x1_free = ostg_v
        ufma_free_t = qfm_free_t
        ufma_free_v = vS.v
        ostg_all = [(sSTp[0], sSTp[0].v), (sSTp[1], sSTp[1].v)]
        xu_extra = [(sTP, tp_f), (sS, ap4_last), (vS, ostg_v)]

    final_waits()


def _host_layout(inputs):
    Wq = np.asarray(inputs["L0_attn_Wq"], np.float32)
    Wkd = np.asarray(inputs["L0_attn_Wk_down"], np.float32)
    Wvd = np.asarray(inputs["L0_attn_Wv_down"], np.float32)
    Wku = np.asarray(inputs["L0_attn_Wk_up"], np.float32)
    Wvu = np.asarray(inputs["L0_attn_Wv_up"], np.float32)
    Wo = np.asarray(inputs["L0_attn_Wo"], np.float32)
    bW = np.asarray(inputs["L0_ffn_base_W"], np.float32)
    gW = np.asarray(inputs["L0_ffn_gate_W"], np.float32)
    lA = np.asarray(inputs["L0_ffn_lora_A"], np.float32)
    lB = np.asarray(inputs["L0_ffn_lora_B"], np.float32)

    C = np.ascontiguousarray
    kv = np.concatenate([Wkd.reshape(8, 32, DM), Wvd.reshape(8, 32, DM)], axis=1)
    kv = kv.reshape(512, DM)
    WKVD4 = C(kv.T.reshape(32, 128, 4, 128).transpose(2, 1, 0, 3).reshape(4, 128, 4096))
    WQ4 = C(Wq.T.reshape(32, 128, 32, 128).transpose(2, 1, 0, 3).reshape(32, 128, 4096))
    WO4 = C(Wo.T.reshape(32, 128, 32, 128).transpose(2, 1, 0, 3).reshape(32, 128, 4096))
    wku = Wku.T.reshape(4, 64, 1024)
    zk = np.zeros_like(wku)
    WKUDE = C(np.concatenate([wku, zk], axis=1).transpose(1, 0, 2).reshape(128, 4 * 1024))
    WKUDO = C(np.concatenate([zk, wku], axis=1).transpose(1, 0, 2).reshape(128, 4 * 1024))
    wvu = Wvu.T.reshape(4, 64, 1024)
    zv = np.zeros_like(wvu)
    WVUDE = C(np.concatenate([wvu, zv], axis=1).transpose(1, 0, 2).reshape(128, 4 * 1024))
    WVUDO = C(np.concatenate([zv, wvu], axis=1).transpose(1, 0, 2).reshape(128, 4 * 1024))
    bw4 = bW.T.reshape(32, 128, 32, 128).transpose(2, 1, 0, 3)
    Bmat = lB.transpose(1, 0, 2).reshape(DM, 224)
    BmT = np.zeros((256, DM), np.float32)
    BmT[0:224, :] = Bmat.T
    bm4 = BmT.reshape(2, 128, 32, 128).transpose(2, 1, 0, 3)
    BWBM4 = C(np.concatenate([bw4, bm4], axis=2).reshape(32, 128, 34 * 128))
    GWT = C(gW.T.reshape(32, 128, NE).transpose(1, 0, 2).reshape(128, 32 * NE))
    ATf = np.zeros((DM, 256), np.float32)
    ATf[:, 0:224] = lA.reshape(224, DM).T
    ATP = C(ATf.reshape(32, 128, 256))
    return dict(WKVD4=WKVD4, WQ4=WQ4, WO4=WO4, WKUDE=WKUDE, WKUDO=WKUDO,
                WVUDE=WVUDE, WVUDO=WVUDO, BWBM4=BWBM4, GWT=GWT, ATP=ATP)


def _thr_for(j):
    thr = np.zeros(64, np.float32)
    for ci in range(2):
        cq = j if ci == 0 else 7 - j
        qb = 512 * cq
        for sl in range(8):
            for kt in range(4):
                thr[ci * 32 + sl * 4 + kt] = 512 * sl + 128 * kt - qb
    return np.ascontiguousarray(np.broadcast_to(thr[None, :], (128, 64)).astype(np.float32))


def kernel(**inputs):
    x = np.asarray(inputs["L0_attn_hidden_states"], np.float32)
    w = _host_layout(inputs)
    nc = bass.Bass("TRN2", target_bir_lowering=False, debug=False)
    build(nc)
    in_maps = []
    for core in range(8):
        b, j = core // 4, core % 4
        cqA, cqB = j, 7 - j
        xb = np.ascontiguousarray(x[b])
        xown = np.ascontiguousarray(
            np.concatenate([x[b, 512 * cqA:512 * (cqA + 1)],
                            x[b, 512 * cqB:512 * (cqB + 1)]], axis=0))
        m = {"X": xb, "XOWN": xown, "THRD": _thr_for(j)}
        m.update(w)
        in_maps.append(m)
    res = run_bass_kernel_spmd(nc, in_maps, core_ids=list(range(8)))
    out = np.empty((2, SEQ, DM), np.float32)
    for core in range(8):
        b, j = core // 4, core % 4
        o = res.results[core]["OUT"]
        out[b, 512 * j:512 * (j + 1)] = o[0:512]
        out[b, 512 * (7 - j):512 * (8 - j)] = o[512:1024]
    return out


# version 25
# speedup vs baseline: 1.0474x; 1.0474x over previous
import math
import numpy as np
import concourse.bass as bass
import concourse.mybir as mybir
from concourse.bass_utils import run_bass_kernel_spmd

F32 = mybir.dt.float32
BF16 = mybir.dt.bfloat16
I32 = mybir.dt.int32
AF = mybir.ActivationFunctionType
AL = mybir.AluOpType
AX = mybir.AxisListType.X

SEQ = 4096
DM = 4096
EPS = 1e-5
SCALE = 1.0 / math.sqrt(128.0)
GSCALE = 2.0
NE = 28

O_KCVC = 17408
O_Q = 50176
O_ACC = 82944
O_X1 = 115712
O_XU = 148480
O_UTM = 164864
O_KV = 181248
O_VTM = 185344
O_PTD = 189440
O_WS = 197632
O_WP1 = 206336
O_M01 = 215040
O_IOTA = 219136
O_THR = 221184
O_HSCT = 221440
O_ZAC = 223488
O_ONES = 225536
O_STAT = 225792
O_LOGTM = 226176
O_GSC = 226432
O_HSC = 226688
O_GST = 227712
O_GWP = 227840
O_E16 = 227968


class Sem:
    __slots__ = ("h", "v")

    def __init__(self, nc, name):
        self.h = nc.alloc_semaphore(name)
        self.v = 0

    def inc(self, ins, k=1):
        ins.then_inc(self.h, k)
        self.v += k
        return self.v


def build(nc, do_b=True, do_c=True, dbg=False, dbg_stage=None):
    X = nc.dram_tensor("X", [SEQ, DM], F32, kind="ExternalInput").ap()
    XOWN = nc.dram_tensor("XOWN", [1024, DM], F32, kind="ExternalInput").ap()
    THRD = nc.dram_tensor("THRD", [128, 64], F32, kind="ExternalInput").ap()
    WKVD4 = nc.dram_tensor("WKVD4", [4, 128, 4096], F32, kind="ExternalInput").ap()
    WQ4 = nc.dram_tensor("WQ4", [32, 128, 4096], F32, kind="ExternalInput").ap()
    WKUDE = nc.dram_tensor("WKUDE", [128, 4 * 1024], F32, kind="ExternalInput").ap()
    WKUDO = nc.dram_tensor("WKUDO", [128, 4 * 1024], F32, kind="ExternalInput").ap()
    WVUDE = nc.dram_tensor("WVUDE", [128, 4 * 1024], F32, kind="ExternalInput").ap()
    WVUDO = nc.dram_tensor("WVUDO", [128, 4 * 1024], F32, kind="ExternalInput").ap()
    WO4 = nc.dram_tensor("WO4", [32, 128, 4096], F32, kind="ExternalInput").ap()
    BWBM4 = nc.dram_tensor("BWBM4", [32, 128, 4352], F32, kind="ExternalInput").ap()
    GWT = nc.dram_tensor("GWT", [128, 896], F32, kind="ExternalInput").ap()
    ATP = nc.dram_tensor("ATP", [32, 128, 256], F32, kind="ExternalInput").ap()
    OUT = nc.dram_tensor("OUT", [1024, DM], F32, kind="ExternalOutput").ap()
    WQB = nc.dram_tensor("WQB", [32, 128, 4096], BF16, kind="Internal").ap()
    WOB = nc.dram_tensor("WOB", [32, 128, 4096], BF16, kind="Internal").ap()
    BWBMB = nc.dram_tensor("BWBMB", [32, 128, 4352], BF16, kind="Internal").ap()
    ATPB = nc.dram_tensor("ATPB", [32, 128, 256], BF16, kind="Internal").ap()
    WKVDB = nc.dram_tensor("WKVDB", [4, 32, 128, 128], BF16, kind="Internal").ap()
    DBGT = None
    if dbg:
        DBGT = nc.dram_tensor("DBG", [512, DM], F32, kind="ExternalOutput").ap()

    A = lambda name, shape, dt, off: nc.alloc_sbuf_tensor_at(name, shape, dt, offset=off)
    KCVC = A("KCVC", [128, 4 * 4096], BF16, O_KCVC)
    QFM = A("QFM", [128, 32 * 512], BF16, O_Q)
    ACC = A("ACC", [128, 16 * 512], F32, O_ACC)
    UFMA = A("UFMA", [128, 32 * 512], BF16, O_ACC)
    OUT2T = A("OUT2T", [128, 32 * 512], BF16, O_ACC)
    FFNT = A("FFNT", [128, 32 * 512], BF16, O_ACC)
    X1 = A("X1", [128, 4 * 4096], BF16, O_X1)
    XA = A("XA", [128, 2 * 4096], BF16, O_XU)
    XA2 = A("XA2", [128, 4 * 4096], BF16, O_Q)
    UFMA2 = A("UFMA2", [128, 32 * 512], BF16, O_X1)
    UTM = A("UTM", [128, 2 * 4096], BF16, O_UTM)
    OUT2TM = A("OUT2TM", [128, 4 * 4096], BF16, O_XU)
    FFNTM = A("FFNTM", [128, 4 * 4096], BF16, O_XU)
    KFM = A("KFM", [128, 4 * 512], BF16, O_KV)
    VTM = A("VTM", [128, 4 * 512], BF16, O_VTM)
    PTD = A("PTD", [128, 4 * 2048], BF16, O_XU)
    Y2 = A("Y2", [128, 4096], BF16, O_PTD)
    Y2B = A("Y2B", [128, 4096], BF16, O_KV)
    OSTG = A("OSTG", [128, 2048], F32, O_KV)
    OSTG2 = A("OSTG2", [128, 2048], F32, O_Q)
    WP = [A("WPa", [128, 34 * 128], BF16, O_WS), A("WPb", [128, 34 * 128], BF16, O_WP1)]
    WKUS = A("WKUS", [128, 4 * 1024], BF16, O_WS)
    WVUS = A("WVUS", [128, 4 * 1024], BF16, O_WP1)
    WKUSO = A("WKUSO", [128, 4 * 1024], BF16, O_UTM)
    WVUSO = A("WVUSO", [128, 4 * 1024], BF16, O_UTM + 8192)
    M01 = A("M01", [128, 4 * 512], BF16, O_M01)
    IND16 = A("IND16", [16, 16 * 128], BF16, O_M01)
    IOTA = A("IOTA", [128, 512], F32, O_IOTA)
    IOTAI = A("IOTAI", [128, 512], I32, O_HSCT)
    THR = A("THR", [128, 64], F32, O_THR)
    HSCT = A("HSCT", [128, 2 * 512], BF16, O_HSCT)
    ZAC = A("ZAC", [128, 512], F32, O_ZAC)
    LOGPAD = A("LOGPAD", [32, 512], BF16, O_ZAC)
    ONES = A("ONES", [128, 128], BF16, O_ONES)
    STAT = A("STAT", [128, 96], F32, O_STAT)
    LOGTM = A("LOGTM", [128, 128], BF16, O_LOGTM)
    GSC = A("GSC", [128, 4 * 32], BF16, O_GSC)
    HSC = A("HSC", [128, 2 * 256], BF16, O_HSC)
    RZB = A("RZB", [16, 512], BF16, O_HSC)
    GST = A("GST", [128, 32], F32, O_GST)
    GWP = A("GWP", [128, 2 * 32], BF16, O_GWP)
    GWPF = A("GWPF", [128, 32 * 28], BF16, O_M01)
    BSTAT = A("BSTAT", [128, 96], F32, 228480)
    E16 = A("E16", [128, 256], BF16, O_E16)

    PSab = [nc.alloc_psum_tensor("psa", [128, 512], F32),
            nc.alloc_psum_tensor("psb", [128, 512], F32)]
    PSS = nc.alloc_psum_tensor("pss", [128, 2048], F32)
    PSZ = nc.alloc_psum_tensor("psz", [128, 512], F32)
    PSV = nc.alloc_psum_tensor("psv", [128, 512], F32)

    TEN, VEC, SCA, GPS, SYN = nc.tensor, nc.vector, nc.scalar, nc.gpsimd, nc.sync

    tS = Sem(nc, "tS")
    vS = Sem(nc, "vS")
    sS = Sem(nc, "sS")
    gS = Sem(nc, "gS")
    sXA = [Sem(nc, "sXA%d" % i) for i in range(6)]
    sWPs = [Sem(nc, "sWP0"), Sem(nc, "sWP1")]
    sWPh = [Sem(nc, "sWPh0"), Sem(nc, "sWPh1")]
    sTU = [Sem(nc, "sTU0"), Sem(nc, "sTU1")]
    sTH = [Sem(nc, "sTH0"), Sem(nc, "sTH1")]
    sTP = Sem(nc, "sTP")
    sTHR = Sem(nc, "sTHR")
    sWKV = Sem(nc, "sWKV")
    sGW = [Sem(nc, "sGW0"), Sem(nc, "sGW1")]
    sAP = [Sem(nc, "sAP0"), Sem(nc, "sAP1")]
    sST = Sem(nc, "sST")
    sSTp = [Sem(nc, "sSTa"), Sem(nc, "sSTb")]
    sRES = Sem(nc, "sRES")
    sCVQ = Sem(nc, "sCVQ")
    sCVO = Sem(nc, "sCVO")
    sCVB = Sem(nc, "sCVB")
    sCVA = Sem(nc, "sCVA")
    sCVK = Sem(nc, "sCVK")

    def W(eng, *sv):
        for s, v in sv:
            if v > 0:
                eng.wait_ge(s.h, v)

    GPS.iota(IOTAI[:], pattern=[[1, 512]], base=0, channel_multiplier=-1)
    GPS.drain()
    GPS.tensor_copy(IOTA[:], IOTAI[:])
    GPS.iota(E16[:].rearrange("p (h i) -> p h i", h=16, i=16),
             pattern=[[-1, 16], [1, 16]], base=0, channel_multiplier=0,
             allow_small_or_imprecise_dtypes=True)
    iota_done = gS.inc(GPS.drain())
    thr_ld = sTHR.inc(GPS.dma_start(out=THR[:], in_=THRD[:]), 16)
    VEC.memset(ONES[:], 1.0)
    VEC.drain()
    W(VEC, (gS, iota_done))
    VEC.tensor_single_scalar(E16[:], E16[:], 0.0, AL.is_equal)
    VEC.drain()

    ln_prev_ap = {0: 0, 1: 0}

    def ln_stats_apply(src_tile, utile, pb, waits_v=(), waits_s=()):
        b = pb * 16
        Sm = STAT[:, b + 0:b + 1]
        Sv = STAT[:, b + 1:b + 2]
        Sq = STAT[:, b + 4:b + 5]
        Svr = STAT[:, b + 5:b + 6]
        Ssd = STAT[:, b + 6:b + 7]
        Sr = STAT[:, b + 7:b + 8]
        Snmr = STAT[:, b + 8:b + 9]
        BS = BSTAT[:, pb * 48:(pb + 1) * 48]
        W(VEC, *waits_v)
        for i in range(8):
            VEC.bn_stats(BS[:, i * 6:(i + 1) * 6], src_tile[:, i * 512:(i + 1) * 512])
        VEC.drain()
        VEC.bn_aggr(STAT[:, b:b + 2], BS)
        VEC.drain()
        VEC.tensor_scalar(Svr, Sv, 1.0, EPS, AL.mult, AL.add)
        VEC.drain()
        W(VEC, (sS, ln_prev_ap[pb]))
        VEC.reciprocal(Ssd, Svr)
        VEC.drain()
        VEC.tensor_single_scalar(Sq, Sm, -1.0, AL.mult)
        var_mark = vS.inc(VEC.drain())
        W(SCA, (vS, var_mark), *waits_s)
        SCA.activation(Sr, Ssd, AF.Sqrt)
        SCA.drain()
        SCA.mul(Snmr, Sr, Sq)
        SCA.drain()
        ins = SCA.activation(utile, src_tile, AF.Identity, bias=Snmr, scale=Sr)
        ap_mark = sS.inc(ins)
        ln_prev_ap[pb] = ap_mark
        return ap_mark, var_mark

    def tr_u(dst, tspan, toff, utile, waits, pb):
        W(SCA, *waits)
        ap = dst[:].rearrange("p (c t) -> p c t", c=32, t=tspan)[:, :, toff:toff + 128]
        return sTU[pb].inc(SCA.dma_start(out=ap, in_=utile, transpose=True), 16)

    all_dma_sems = sXA + [sWPs[0], sWPs[1], sWPh[0], sWPh[1],
                    sTU[0], sTU[1], sTH[0], sTH[1],
                    sTP, sTHR, sWKV, sGW[0], sGW[1], sAP[0], sAP[1], sST, sRES,
                    sSTp[0], sSTp[1], sCVQ, sCVO, sCVB, sCVA, sCVK]

    def final_waits():
        for s in all_dma_sems:
            if s.v > 0:
                SYN.wait_ge(s.h, s.v)

    conv_jobs = []
    for i in range(32):
        conv_jobs.append((sCVQ, WQB[i], WQ4[i]))
    for i in range(32):
        conv_jobs.append((sCVO, WOB[i], WO4[i]))
    for i in range(32):
        conv_jobs.append((sCVB, BWBMB[i], BWBM4[i]))
    for i in range(32):
        conv_jobs.append((sCVA, ATPB[i], ATP[i]))
    cv_i = [0]

    def pump_conv(n):
        for _ in range(n):
            if cv_i[0] >= len(conv_jobs):
                return
            s, d, srcp = conv_jobs[cv_i[0]]
            s.inc(GPS.dma_start(out=d, in_=srcp), 16)
            cv_i[0] += 1

    Xr = X.rearrange("(q0 eight) d -> q0 eight d", eight=8)
    xu_extra = []
    free_xa = [(0, 0)] * 6

    def xa_tile(i):
        if i < 2:
            return XA[:, i * 4096:(i + 1) * 4096]
        return XA2[:, (i - 2) * 4096:(i - 1) * 4096]
    utm_free = [0, 0]
    ufma_free_t = 0
    ufma_free_v = 0
    wp_free = [0, 0]
    psab_free = [0, 0]
    ufma_pa = {0: 0, 1: 0}
    pend_xm = {}

    def pa_loads(ub):
        for t4 in range(4):
            b = (ub * 4 + t4) % 6
            sv, _ = free_xa[b]
            W(GPS, (sS, sv))
            src = Xr[t4 * 128:(t4 + 1) * 128, ub, :]
            pend_xm[t4] = sXA[b].inc(GPS.dma_start(out=xa_tile(b), in_=src), 16)

    pa_loads(0)
    for ub in range(8):
        ubuf = UFMA if ub % 2 == 0 else UFMA2
        for t4 in range(4):
            pb = (ub * 4 + t4) % 6
            pl = (ub * 4 + t4) % 2
            xm = pend_xm[t4]
            xt = xa_tile(pb)
            ut = UTM[:, pl * 4096:(pl + 1) * 4096]
            ap_mark, red_mark = ln_stats_apply(
                xt, ut, pl,
                waits_v=[(sXA[pb], xm)],
                waits_s=[(sXA[pb], xm), (sTU[pl], utm_free[pl])])
            free_xa[pb] = (ap_mark, red_mark)
            pump_conv(3)
            tr_u(ubuf, 512, t4 * 128, ut,
                 [(sS, ap_mark), (tS, ufma_pa[ub % 2]), (vS, ufma_free_v)], pl)
            utm_free[pl] = sTU[pl].v
        if ub < 7:
            pa_loads(ub + 1)
        tp_all = [(sTU[0], sTU[0].v), (sTU[1], sTU[1].v)]
        for cg in range(4):
            wp = cg % 2
            W(GPS, (tS, wp_free[wp]))
            wm = sWPs[wp].inc(GPS.dma_start(out=WP[wp][:, 0:32 * 128], in_=WKVD4[cg]), 16)
            pp = cg % 2
            W(TEN, (sWPs[wp], wm), (vS, psab_free[pp]), *tp_all)
            for c in range(32):
                ins = TEN.matmul(PSab[pp][:], WP[wp][:, c * 128:(c + 1) * 128],
                                 ubuf[:, c * 512:(c + 1) * 512],
                                 start=(c == 0), stop=(c == 31))
            mm_mark = tS.inc(ins)
            wp_free[wp] = mm_mark
            if cg == 3:
                ufma_pa[ub % 2] = mm_mark
            W(VEC, (tS, mm_mark))
            psab_free[pp] = vS.inc(VEC.tensor_copy(
                KCVC[:, cg * 4096 + ub * 512:cg * 4096 + (ub + 1) * 512], PSab[pp][:]))
    ufma_free_t = ufma_pa[0]
    x1_pa = [(tS, ufma_pa[1])]

    if not do_b:
        stm = 0
        for cc in range(4):
            W(VEC, (sST, stm))
            VEC.tensor_copy(OSTG[:], KCVC[:, cc * 4096:(cc + 1) * 4096])
            dm = vS.inc(VEC.drain())
            W(SYN, (vS, dm))
            stm = sST.inc(SYN.dma_start(out=DBGT[cc * 128:(cc + 1) * 128, :], in_=OSTG[:]), 16)
            SYN.wait_ge(sST.h, stm)
        final_waits()
        return

    kcvc_ready = vS.v
    ostg_all = []
    kv_free_t = 0
    y2_free_t = 0
    x1_free = 0
    qfm_free_t = 0

    for ci in range(2):
        nsl = 4 if ci == 0 else 8
        xbase = ci * 512
        for tt in range(4):
            pb = (32 + ci * 4 + tt) % 6
            pl = tt % 2
            sv, vv = free_xa[pb]
            W(GPS, (sS, sv), (tS, kv_free_t), *xu_extra, *ostg_all)
            src = XOWN[xbase + tt * 128: xbase + (tt + 1) * 128, :]
            xm = sXA[pb].inc(GPS.dma_start(out=xa_tile(pb), in_=src), 16)
            xt = xa_tile(pb)
            ut = UTM[:, pl * 4096:(pl + 1) * 4096]
            ap_mark, red_mark = ln_stats_apply(
                xt, ut, pl,
                waits_v=[(sXA[pb], xm)],
                waits_s=[(sXA[pb], xm), (sTU[pl], utm_free[pl])] + xu_extra)
            free_xa[pb] = (ap_mark, red_mark)
            tr_u(UFMA, 512, tt * 128, ut,
                 [(sS, ap_mark), (tS, ufma_free_t), (vS, ufma_free_v)], pl)
            utm_free[pl] = sTU[pl].v
        tp_all = [(sTU[0], sTU[0].v), (sTU[1], sTU[1].v)]
        pump_conv(len(conv_jobs))
        for hdc in range(32):
            wp = hdc % 2
            W(SYN, (tS, wp_free[wp]), (sCVQ, sCVQ.v))
            wm = sWPh[wp].inc(SYN.dma_start(out=WP[wp][:, 0:32 * 128], in_=WQB[hdc]), 16)
            pp = hdc % 2
            W(TEN, (sWPh[wp], wm), (vS, psab_free[pp]), *tp_all)
            for c in range(32):
                ins = TEN.matmul(PSab[pp][:], WP[wp][:, c * 128:(c + 1) * 128],
                                 UFMA[:, c * 512:(c + 1) * 512],
                                 start=(c == 0), stop=(c == 31))
            mm_mark = tS.inc(ins)
            wp_free[wp] = mm_mark
            if hdc == 31:
                ufma_free_t = mm_mark
            W(VEC, (tS, mm_mark))
            if hdc == 0:
                W(VEC, *ostg_all)
            psab_free[pp] = vS.inc(VEC.tensor_copy(QFM[:, hdc * 512:(hdc + 1) * 512],
                                                   PSab[pp][:]))
        q_ready = vS.v

        W(GPS, (tS, max(wp_free)))
        sWKV.inc(GPS.dma_start(out=WKUS[:], in_=WKUDE[:]), 16)
        sWKV.inc(GPS.dma_start(out=WKUSO[:], in_=WKUDO[:]), 16)
        sWKV.inc(GPS.dma_start(out=WVUS[:], in_=WVUDE[:]), 16)
        wkv_ld = sWKV.inc(GPS.dma_start(out=WVUSO[:], in_=WVUDO[:]), 16)
        W(GPS, (vS, x1_free), *x1_pa)
        x1_pa = []
        xl_last = 0
        for tt in range(4):
            src = XOWN[xbase + tt * 128: xbase + (tt + 1) * 128, :]
            xl_last = sRES.inc(GPS.dma_start(
                out=X1[:, tt * 4096:(tt + 1) * 4096], in_=src), 16)

        for hg in range(2):
            for sl in range(nsl):
                W(VEC, (sTHR, thr_ld), (gS, iota_done), *ostg_all)
                for kt in range(4):
                    col = ci * 32 + sl * 4 + kt
                    VEC.tensor_single_scalar(M01[:, kt * 512:(kt + 1) * 512], IOTA[:],
                                             THR[:, col:col + 1], AL.is_ge)
                mask_mark = vS.inc(VEC.drain())
                base = (sl // 2) * 4096
                wkp = WKUS if sl % 2 == 0 else WKUSO
                wvp = WVUS if sl % 2 == 0 else WVUSO
                for kv4 in range(4):
                    hk2 = hg * 4 + kv4
                    pp = kv4 % 2
                    W(TEN, (sWKV, wkv_ld), (vS, max(psab_free[pp], kcvc_ready)))
                    for u in range(4):
                        ins = TEN.matmul(
                            PSab[pp][:],
                            wkp[:, u * 1024 + hk2 * 128: u * 1024 + (hk2 + 1) * 128],
                            KCVC[:, base + u * 512: base + (u + 1) * 512],
                            start=(u == 0), stop=(u == 3))
                    km = tS.inc(ins)
                    W(VEC, (tS, km))
                    psab_free[pp] = vS.inc(VEC.tensor_copy(
                        KFM[:, kv4 * 512:(kv4 + 1) * 512], PSab[pp][:]))
                for kt in range(4):
                    pp = kt % 2
                    W(TEN, (vS, psab_free[pp]))
                    for u in range(4):
                        ins = TEN.matmul(
                            PSab[pp][:],
                            KCVC[:, base + (u + 4) * 512 + kt * 128:
                                    base + (u + 4) * 512 + (kt + 1) * 128],
                            wvp[:, u * 1024 + hg * 512: u * 1024 + (hg + 1) * 512],
                            start=(u == 0), stop=(u == 3))
                    vm = tS.inc(ins)
                    wkus_last_t = vm
                    W(VEC, (tS, vm))
                    psab_free[pp] = vS.inc(VEC.tensor_copy(
                        VTM[:, kt * 512:(kt + 1) * 512], PSab[pp][:]))
                upv_done = vS.v
                if dbg_stage == "kv0" and ci == 0 and hg == 0 and sl == 0:
                    stm = 0
                    DSTG = ACC[:, 0:2048]
                    for (src_t, row) in ((KFM, 0), (VTM, 128)):
                        W(VEC, (sST, stm), (vS, upv_done))
                        VEC.tensor_copy(DSTG, src_t[:, 0:2048])
                        dm = vS.inc(VEC.drain())
                        W(SYN, (vS, dm))
                        stm = sST.inc(SYN.dma_start(
                            out=DBGT[row:row + 128, 0:2048], in_=DSTG), 16)
                        SYN.wait_ge(sST.h, stm)
                    final_waits()
                    return

                pss_free = [0, 0]
                ptfree = [0, 0, 0, 0]
                pends = []
                for h4 in range(19):
                    if len(pends) == 3 or (h4 >= 16 and pends):
                        ph4, pptp, pkv4, pmult = pends.pop(0)
                        avb = ph4 % 2
                        W(TEN, (vS, max(pmult, psab_free[avb])))
                        for kt in range(4):
                            TEN.matmul(PSZ[0:16, :],
                                       E16[:, ph4 * 16:(ph4 + 1) * 16],
                                       PTD[:, pptp * 2048 + kt * 512:
                                           pptp * 2048 + (kt + 1) * 512],
                                       start=(sl == 0 and ph4 == 0 and kt == 0),
                                       stop=(sl == nsl - 1 and ph4 == 15 and kt == 3))
                        for kt in range(4):
                            ins = TEN.matmul(
                                PSab[avb][:],
                                VTM[:, kt * 512 + pkv4 * 128: kt * 512 + (pkv4 + 1) * 128],
                                PTD[:, pptp * 2048 + kt * 512: pptp * 2048 + (kt + 1) * 512],
                                start=(kt == 0), stop=(kt == 3))
                        zav_mark = tS.inc(ins)
                        ptfree[pptp] = zav_mark
                        if ph4 == 15 and sl == nsl - 1 and hg == 1:
                            kv_free_t = zav_mark
                            y2_free_t = zav_mark
                        W(VEC, (tS, zav_mark))
                        if sl == 0:
                            ins = VEC.tensor_copy(ACC[:, ph4 * 512:(ph4 + 1) * 512],
                                                  PSab[avb][:])
                        else:
                            ins = VEC.tensor_tensor(ACC[:, ph4 * 512:(ph4 + 1) * 512],
                                                    ACC[:, ph4 * 512:(ph4 + 1) * 512],
                                                    PSab[avb][:], AL.add)
                        psab_free[avb] = vS.inc(ins)
                    if h4 < 16:
                        h = hg * 16 + h4
                        kv4 = h4 // 4
                        ptp = h4 % 4
                        pfs = ptfree[ptp]
                        sc_half = []
                        for hf2 in range(2):
                            W(TEN, (sS, pss_free[hf2]), (vS, upv_done if h4 == 0 else 0))
                            for kt in (hf2 * 2, hf2 * 2 + 1):
                                ins = TEN.matmul(
                                    PSS[:, kt * 512:(kt + 1) * 512],
                                    KFM[:, kv4 * 512 + kt * 128: kv4 * 512 + (kt + 1) * 128],
                                    QFM[:, h * 512:(h + 1) * 512],
                                    start=True, stop=True)
                            sc_half.append(tS.inc(ins))
                        for hf2 in range(2):
                            W(SCA, (tS, max(sc_half[hf2], pfs)))
                            ins = SCA.activation(
                                PTD[:, ptp * 2048 + hf2 * 1024: ptp * 2048 + (hf2 + 1) * 1024],
                                PSS[:, hf2 * 1024:(hf2 + 1) * 1024],
                                AF.Exp, scale=SCALE)
                            pss_free[hf2] = sS.inc(ins)
                        W(VEC, (sS, pss_free[1]), (vS, mask_mark))
                        pm = vS.inc(VEC.tensor_tensor(PTD[:, ptp * 2048:(ptp + 1) * 2048],
                                                      PTD[:, ptp * 2048:(ptp + 1) * 2048],
                                                      M01[:], AL.mult))
                        pends.append((h4, ptp, kv4, pm))
            W(GPS, (vS, vS.v))
            GPS.iota(IND16[:].rearrange("p (h m) -> p h m", h=16, m=128),
                     pattern=[[1, 16], [0, 128]], base=0, channel_multiplier=-1,
                     allow_small_or_imprecise_dtypes=True)
            ind_g = gS.inc(GPS.drain())
            W(VEC, (gS, ind_g), (tS, zav_mark))
            VEC.tensor_single_scalar(IND16[:], IND16[:], 0.0, AL.is_equal)
            VEC.drain()
            VEC.reciprocal(ZAC[0:16, :], PSZ[0:16, :])
            VEC.drain()
            VEC.tensor_copy(RZB[:], ZAC[0:16, :])
            div_mark = vS.inc(VEC.drain())
            for h4 in range(16):
                h = hg * 16 + h4
                pp = h4 % 2
                W(TEN, (vS, max(div_mark, psab_free[pp])))
                ins = TEN.matmul(PSab[pp][:], IND16[:, h4 * 128:(h4 + 1) * 128],
                                 RZB[:], start=True, stop=True)
                rm = tS.inc(ins)
                W(VEC, (tS, rm))
                psab_free[pp] = vS.inc(VEC.tensor_tensor(QFM[:, h * 512:(h + 1) * 512],
                                                         ACC[:, h4 * 512:(h4 + 1) * 512],
                                                         PSab[pp][:], AL.mult))
        ufma_free_v = vS.v
        attn_ready = vS.v
        pss_free_s = sS.v
        W(GPS, (vS, attn_ready), (tS, tS.v))
        gwf_ld = sGW[0].inc(GPS.dma_start(out=GWPF[:], in_=GWT[:]), 16)
        if dbg_stage == "attnT":
            stm = 0
            for cc in range(4):
                W(VEC, (sST, stm))
                VEC.tensor_copy(OSTG[:], QFM[:, cc * 4096 + 0: cc * 4096 + 2048])
                dm = vS.inc(VEC.drain())
                W(SYN, (vS, dm))
                stm = sST.inc(SYN.dma_start(
                    out=DBGT[cc * 128:(cc + 1) * 128, 0:2048], in_=OSTG[:]), 16)
                SYN.wait_ge(sST.h, stm)
                W(VEC, (sST, stm))
                VEC.tensor_copy(OSTG[:], QFM[:, cc * 4096 + 2048: cc * 4096 + 4096])
                dm = vS.inc(VEC.drain())
                W(SYN, (vS, dm))
                stm = sST.inc(SYN.dma_start(
                    out=DBGT[cc * 128:(cc + 1) * 128, 2048:4096], in_=OSTG[:]), 16)
                SYN.wait_ge(sST.h, stm)
            final_waits()
            return

        wo_base_free = max(max(wp_free), wkus_last_t)
        evac_marks = []
        xu_free_s = sS.v

        def tp_out2(oc2):
            W(SYN, (vS, evac_marks[oc2]), (sS, xu_free_s))
            dstw = OUT2TM[:].rearrange("p (t f) -> p t f", t=4, f=4096)[
                :, :, oc2 * 128:(oc2 + 1) * 128]
            sTP.inc(SYN.dma_start(out=dstw, in_=OUT2T[:, oc2 * 512:(oc2 + 1) * 512],
                                  transpose=True), 16)

        for oc2 in range(32):
            wp = oc2 % 2
            W(SYN, (tS, max(wp_free[wp], wo_base_free)), (sCVO, sCVO.v))
            wm = sWPh[wp].inc(SYN.dma_start(out=WP[wp][:, 0:32 * 128], in_=WOB[oc2]), 16)
            pp = oc2 % 2
            W(TEN, (sWPh[wp], wm), (vS, max(psab_free[pp], attn_ready)))
            for hdc in range(32):
                ins = TEN.matmul(PSab[pp][:], WP[wp][:, hdc * 128:(hdc + 1) * 128],
                                 QFM[:, hdc * 512:(hdc + 1) * 512],
                                 start=(hdc == 0), stop=(hdc == 31))
            mm_mark = tS.inc(ins)
            wp_free[wp] = mm_mark
            if oc2 == 31:
                qfm_free_t = mm_mark
            W(VEC, (tS, mm_mark))
            em = vS.inc(VEC.tensor_copy(OUT2T[:, oc2 * 512:(oc2 + 1) * 512], PSab[pp][:]))
            psab_free[pp] = em
            evac_marks.append(em)
            if oc2 >= 2:
                tp_out2(oc2 - 2)
        tp_out2(30)
        tp_out2(31)
        tp_o2 = sTP.v
        y2r = [0, 0]
        add_marks = []
        for tt in range(4):
            par = tt % 2
            ybuf = Y2 if par == 0 else Y2B
            o2t = OUT2TM[:, tt * 4096:(tt + 1) * 4096]
            ap_mark, _ = ln_stats_apply(
                o2t, ybuf[:], par,
                waits_v=[(sTP, tp_o2)],
                waits_s=[(sTP, tp_o2), (tS, kv_free_t), (vS, y2r[par])])
            W(VEC, (sS, ap_mark), (sRES, xl_last))
            ins = VEC.tensor_tensor(X1[:, tt * 4096:(tt + 1) * 4096],
                                    X1[:, tt * 4096:(tt + 1) * 4096], ybuf[:], AL.add)
            y2r[par] = vS.inc(ins)
            add_marks.append(y2r[par])
        res_free_v = add_marks[3]

        if not do_c:
            stm = 0
            for tt in range(4):
                for hf in range(2):
                    W(VEC, (sST, stm), (vS, res_free_v))
                    VEC.tensor_copy(OSTG[:],
                                    X1[:, tt * 4096 + hf * 2048: tt * 4096 + (hf + 1) * 2048])
                    dm = vS.inc(VEC.drain())
                    W(SYN, (vS, dm))
                    stm = sST.inc(SYN.dma_start(
                        out=DBGT[tt * 128:(tt + 1) * 128, hf * 2048:(hf + 1) * 2048],
                        in_=OSTG[:]), 16)
                    SYN.wait_ge(sST.h, stm)
            final_waits()
            return

        for tt in range(4):
            par = tt % 2
            ybuf = Y2 if par == 0 else Y2B
            x1t = X1[:, tt * 4096:(tt + 1) * 4096]
            ap_mark, _ = ln_stats_apply(
                x1t, ybuf[:], par,
                waits_v=[(vS, add_marks[tt])],
                waits_s=[(sTU[par], sTU[par].v),
                         (vS, max(y2r[par], add_marks[tt]))] + ostg_all)
            tr_u(QFM, 512, tt * 128, ybuf[:], [(sS, ap_mark), (tS, qfm_free_t)], par)
        tp_u1 = [(sTU[0], sTU[0].v), (sTU[1], sTU[1].v)]
        for c in range(32):
            if c == 0:
                W(TEN, (sGW[0], gwf_ld), (vS, psab_free[0]), *tp_u1)
            ins = TEN.matmul(PSab[0][0:28, :], GWPF[:, c * 28:(c + 1) * 28],
                             QFM[:, c * 512:(c + 1) * 512], start=(c == 0), stop=(c == 31))
        gate_mm = tS.inc(ins)
        W(VEC, (tS, gate_mm))
        VEC.memset(LOGPAD[:], -30000.0)
        VEC.drain()
        VEC.tensor_copy(LOGPAD[0:28, :], PSab[0][0:28, :])
        lg_mark = vS.inc(VEC.drain())
        psab_free[0] = lg_mark
        W(SYN, (vS, lg_mark))
        tp_log = sTP.inc(SYN.dma_start(
            out=LOGTM[:].rearrange("p (a b) -> p a b", a=4, b=32),
            in_=LOGPAD[:], transpose=True), 16)
        for tt in range(4):
            L28 = LOGTM[:, tt * 32: tt * 32 + 28]
            m1 = GST[:, 0:1]
            m2 = GST[:, 1:2]
            nm1 = GST[:, 2:3]
            zg = GST[:, 3:4]
            rg = GST[:, 4:5]
            msk = STAT[:, 32:60]
            l2 = STAT[:, 64:92]
            W(VEC, (sTP, tp_log))
            VEC.reduce_max(m1, L28, axis=AX)
            VEC.drain()
            VEC.tensor_single_scalar(msk, L28, m1, AL.is_ge)
            VEC.drain()
            VEC.scalar_tensor_tensor(l2, msk, -30000.0, L28, AL.mult, AL.add)
            VEC.drain()
            VEC.reduce_max(m2, l2, axis=AX)
            VEC.drain()
            VEC.tensor_single_scalar(msk, L28, m2, AL.is_ge)
            VEC.drain()
            VEC.tensor_single_scalar(nm1, m1, -1.0, AL.mult)
            g1 = vS.inc(VEC.drain())
            W(SCA, (vS, g1))
            ge = sS.inc(SCA.activation(l2, L28, AF.Exp, bias=nm1, scale=1.0))
            W(VEC, (sS, ge))
            VEC.tensor_tensor(l2, l2, msk, AL.mult)
            VEC.drain()
            VEC.reduce_sum(zg, l2, axis=AX)
            VEC.drain()
            VEC.reciprocal(rg, zg)
            VEC.drain()
            VEC.tensor_scalar(GSC[:, tt * 32: tt * 32 + 28], l2, rg, GSCALE,
                              AL.mult, AL.mult)
            vS.inc(VEC.drain())
        gate_done = vS.v
        ap_free = [wp_free[0], wp_free[0]]
        for c in range(32):
            ap2 = c % 2
            W(SYN, (tS, ap_free[ap2]), (sCVA, sCVA.v))
            am = sAP[ap2].inc(SYN.dma_start(out=WP[0][:, ap2 * 256:(ap2 + 1) * 256],
                                            in_=ATPB[c]), 16)
            W(TEN, (sAP[ap2], am))
            if c == 0:
                W(TEN, (sS, pss_free_s))
            for ts in range(4):
                ins = TEN.matmul(PSS[:, ts * 512: ts * 512 + 256],
                                 QFM[:, c * 512 + ts * 128: c * 512 + (ts + 1) * 128],
                                 WP[0][:, ap2 * 256:(ap2 + 1) * 256],
                                 start=(c == 0), stop=(c == 31))
            ap_free[ap2] = tS.inc(ins)
        h_mm = tS.v
        hsc_free = [0, 0]
        hsc_marks = []
        for ts in range(4):
            hb = ts % 2
            gbx = GSC[:].rearrange("p (t e) -> p t e", t=4, e=32)[:, ts, 0:28]
            gbx = gbx.rearrange("p (e one) -> p e one", one=1).broadcast_to([128, 28, 8])
            W(VEC, (tS, h_mm), (sTH[hb], hsc_free[hb]), (vS, gate_done))
            VEC.tensor_tensor(
                HSC[:].rearrange("p (b e r) -> p b e r", b=2, e=32, r=8)[:, hb, 0:28, :],
                PSS[:].rearrange("p (t f) -> p t f", t=4, f=512)[:, ts, 0:224]
                      .rearrange("p (e r) -> p e r", e=28, r=8),
                gbx, AL.mult)
            VEC.drain()
            VEC.tensor_copy(HSC[:, hb * 256 + 224: hb * 256 + 256],
                            PSS[:, ts * 512 + 224: ts * 512 + 256])
            hm = vS.inc(VEC.drain())
            hsc_marks.append(hm)
            W(SYN, (vS, hm))
            dstw = HSCT[:].rearrange("p (cc t) -> p cc t", cc=2, t=512)[
                :, :, ts * 128:(ts + 1) * 128]
            hsc_free[hb] = sTH[hb].inc(SYN.dma_start(
                out=dstw, in_=HSC[:, hb * 256:(hb + 1) * 256], transpose=True), 16)
        hsc_last_v = vS.v
        tp_h = [(sTH[0], sTH[0].v), (sTH[1], sTH[1].v)]
        evac_marks = []
        bl_base_free = max(max(wp_free), ap_free[0], ap_free[1])
        xu_free_s = sS.v

        def tp_ffn(oc2):
            W(SYN, (vS, evac_marks[oc2]), (sS, xu_free_s))
            dstw = FFNTM[:].rearrange("p (t f) -> p t f", t=4, f=4096)[
                :, :, oc2 * 128:(oc2 + 1) * 128]
            sTP.inc(SYN.dma_start(out=dstw, in_=FFNT[:, oc2 * 512:(oc2 + 1) * 512],
                                  transpose=True), 16)

        for oc2 in range(32):
            wp = oc2 % 2
            W(SYN, (tS, max(wp_free[wp], bl_base_free)), (sCVB, sCVB.v))
            wm = sWPh[wp].inc(SYN.dma_start(out=WP[wp][:], in_=BWBMB[oc2]), 16)
            pp = oc2 % 2
            W(TEN, (sWPh[wp], wm), (vS, psab_free[pp]))
            for c in range(32):
                TEN.matmul(PSab[pp][:], WP[wp][:, c * 128:(c + 1) * 128],
                           QFM[:, c * 512:(c + 1) * 512], start=(c == 0), stop=False)
            if oc2 == 0:
                W(TEN, *tp_h)
            for cc in range(2):
                ins = TEN.matmul(PSab[pp][:], WP[wp][:, (32 + cc) * 128:(33 + cc) * 128],
                                 HSCT[:, cc * 512:(cc + 1) * 512],
                                 start=False, stop=(cc == 1))
            mm_mark = tS.inc(ins)
            wp_free[wp] = mm_mark
            if oc2 == 31:
                qfm_free_t = mm_mark
            W(VEC, (tS, mm_mark))
            em = vS.inc(VEC.tensor_copy(FFNT[:, oc2 * 512:(oc2 + 1) * 512], PSab[pp][:]))
            psab_free[pp] = em
            evac_marks.append(em)
            if oc2 >= 2:
                tp_ffn(oc2 - 2)
        tp_ffn(30)
        tp_ffn(31)
        tp_f = sTP.v
        ostg_v = 0
        ap4_last = 0
        ostg_prev = [0, 0]
        for tt in range(4):
            ft = FFNTM[:, tt * 4096:(tt + 1) * 4096]
            ap_mark, _ = ln_stats_apply(
                ft, Y2[:], tt % 2,
                waits_v=[(sTP, tp_f), (vS, ostg_v)],
                waits_s=[(sTP, tp_f), (vS, ostg_v)])
            ap4_last = ap_mark
            for hf in range(2):
                ob = (tt * 2 + hf) % 2
                stg = OSTG if ob == 0 else OSTG2
                W(VEC, (sS, ap_mark), (sSTp[ob], ostg_prev[ob]))
                ins = VEC.tensor_tensor(
                    stg[:],
                    X1[:, tt * 4096 + hf * 2048: tt * 4096 + (hf + 1) * 2048],
                    Y2[:, hf * 2048:(hf + 1) * 2048], AL.add)
                ostg_v = vS.inc(ins)
                W(SYN, (vS, ostg_v))
                ostg_prev[ob] = sSTp[ob].inc(SYN.dma_start(
                    out=OUT[xbase + tt * 128: xbase + (tt + 1) * 128,
                            hf * 2048:(hf + 1) * 2048], in_=stg[:]), 16)
        x1_free = ostg_v
        ufma_free_t = qfm_free_t
        ufma_free_v = vS.v
        ostg_all = [(sSTp[0], sSTp[0].v), (sSTp[1], sSTp[1].v)]
        xu_extra = [(sTP, tp_f), (sS, ap4_last), (vS, ostg_v)]

    final_waits()


def _host_layout(inputs):
    Wq = np.asarray(inputs["L0_attn_Wq"], np.float32)
    Wkd = np.asarray(inputs["L0_attn_Wk_down"], np.float32)
    Wvd = np.asarray(inputs["L0_attn_Wv_down"], np.float32)
    Wku = np.asarray(inputs["L0_attn_Wk_up"], np.float32)
    Wvu = np.asarray(inputs["L0_attn_Wv_up"], np.float32)
    Wo = np.asarray(inputs["L0_attn_Wo"], np.float32)
    bW = np.asarray(inputs["L0_ffn_base_W"], np.float32)
    gW = np.asarray(inputs["L0_ffn_gate_W"], np.float32)
    lA = np.asarray(inputs["L0_ffn_lora_A"], np.float32)
    lB = np.asarray(inputs["L0_ffn_lora_B"], np.float32)

    C = np.ascontiguousarray
    kv = np.concatenate([Wkd.reshape(8, 32, DM), Wvd.reshape(8, 32, DM)], axis=1)
    kv = kv.reshape(512, DM)
    WKVD4 = C(kv.T.reshape(32, 128, 4, 128).transpose(2, 1, 0, 3).reshape(4, 128, 4096))
    WQ4 = C(Wq.T.reshape(32, 128, 32, 128).transpose(2, 1, 0, 3).reshape(32, 128, 4096))
    WO4 = C(Wo.T.reshape(32, 128, 32, 128).transpose(2, 1, 0, 3).reshape(32, 128, 4096))
    wku = Wku.T.reshape(4, 64, 1024)
    zk = np.zeros_like(wku)
    WKUDE = C(np.concatenate([wku, zk], axis=1).transpose(1, 0, 2).reshape(128, 4 * 1024))
    WKUDO = C(np.concatenate([zk, wku], axis=1).transpose(1, 0, 2).reshape(128, 4 * 1024))
    wvu = Wvu.T.reshape(4, 64, 1024)
    zv = np.zeros_like(wvu)
    WVUDE = C(np.concatenate([wvu, zv], axis=1).transpose(1, 0, 2).reshape(128, 4 * 1024))
    WVUDO = C(np.concatenate([zv, wvu], axis=1).transpose(1, 0, 2).reshape(128, 4 * 1024))
    bw4 = bW.T.reshape(32, 128, 32, 128).transpose(2, 1, 0, 3)
    Bmat = lB.transpose(1, 0, 2).reshape(DM, 224)
    BmT = np.zeros((256, DM), np.float32)
    BmT[0:224, :] = Bmat.T
    bm4 = BmT.reshape(2, 128, 32, 128).transpose(2, 1, 0, 3)
    BWBM4 = C(np.concatenate([bw4, bm4], axis=2).reshape(32, 128, 34 * 128))
    GWT = C(gW.T.reshape(32, 128, NE).transpose(1, 0, 2).reshape(128, 32 * NE))
    ATf = np.zeros((DM, 256), np.float32)
    ATf[:, 0:224] = lA.reshape(224, DM).T
    ATP = C(ATf.reshape(32, 128, 256))
    return dict(WKVD4=WKVD4, WQ4=WQ4, WO4=WO4, WKUDE=WKUDE, WKUDO=WKUDO,
                WVUDE=WVUDE, WVUDO=WVUDO, BWBM4=BWBM4, GWT=GWT, ATP=ATP)


def _thr_for(j):
    thr = np.zeros(64, np.float32)
    for ci in range(2):
        cq = j if ci == 0 else 7 - j
        qb = 512 * cq
        for sl in range(8):
            for kt in range(4):
                thr[ci * 32 + sl * 4 + kt] = 512 * sl + 128 * kt - qb
    return np.ascontiguousarray(np.broadcast_to(thr[None, :], (128, 64)).astype(np.float32))


def kernel(**inputs):
    x = np.asarray(inputs["L0_attn_hidden_states"], np.float32)
    w = _host_layout(inputs)
    nc = bass.Bass("TRN2", target_bir_lowering=False, debug=False)
    build(nc)
    in_maps = []
    for core in range(8):
        b, j = core // 4, core % 4
        cqA, cqB = j, 7 - j
        xb = np.ascontiguousarray(x[b])
        xown = np.ascontiguousarray(
            np.concatenate([x[b, 512 * cqA:512 * (cqA + 1)],
                            x[b, 512 * cqB:512 * (cqB + 1)]], axis=0))
        m = {"X": xb, "XOWN": xown, "THRD": _thr_for(j)}
        m.update(w)
        in_maps.append(m)
    res = run_bass_kernel_spmd(nc, in_maps, core_ids=list(range(8)))
    out = np.empty((2, SEQ, DM), np.float32)
    for core in range(8):
        b, j = core // 4, core % 4
        o = res.results[core]["OUT"]
        out[b, 512 * j:512 * (j + 1)] = o[0:512]
        out[b, 512 * (7 - j):512 * (8 - j)] = o[512:1024]
    return out
